# version 02
# baseline (speedup 1.0000x reference)
import numpy as np
import ml_dtypes
import concourse.bass as bass
import concourse.mybir as mybir
from concourse.bass_utils import run_bass_kernel_spmd

F32 = mybir.dt.float32
BF16 = mybir.dt.bfloat16
FP8 = mybir.dt.float8e4
AX = mybir.AxisListType.X
AXC = mybir.AxisListType.C
ALU = mybir.AluOpType
AF = mybir.ActivationFunctionType

MAGIC = float(np.float32(1.5 * 2 ** 23))

B = 2
N = 2048
D = 1024
H = 16
HD = 64
INNER = 4096
NT = N // 128
TOK = B * N
TT = TOK // 128
EPS = 1e-5
CHUNK = 256
NCH = TOK // CHUNK
QTC = CHUNK // 128

FAR_O = [384, 512, 768, 1024, 1536]
OFFSETS = sorted(set(range(33)) | {48, 64, 96, 128, 192, 256, 384, 512, 768, 1024, 1536})
WN = 384
DNW = 257
CW = 264
NFAR = 5
INVS = 0.125
GWP = 512

DBG_FFN_STOP = None
DBG_PREP = None
DBG_B = None
DBG_AGF = None


class K:
    ENGINES = ("sync", "vector", "scalar", "tensor", "gpsimd")
    QSEM = {"sync": "dS", "scalar": "dA", "gpsimd": "dG"}
    ESEM = {"vector": "eV", "scalar": "eS", "tensor": "eT", "gpsimd": "eG"}

    def __init__(self, nc):
        self.nc = nc
        self.events = {e: [] for e in self.ENGINES}
        self.sems = {}
        self._semctx = []
        self._scopes = [[]]
        self.wr = {}
        self.rd = {}
        self.waited = {e: {} for e in self.ENGINES}
        for s in ("dS", "dG", "dA", "eV", "eS", "eT", "eG"):
            g = nc.semaphore(s)
            h = g.__enter__()
            self._semctx.append(g)
            self.sems[s] = [h, 0]

    def sb(self, name, shape, dtype, **kw):
        self._uid = getattr(self, "_uid", 0) + 1
        g = self.nc.sbuf_tensor(f"{name}_u{self._uid}", shape, dtype, **kw)
        t = g.__enter__()
        self._scopes[-1].append(g)
        return t

    def ps(self, name, shape, dtype=F32):
        self._uid = getattr(self, "_uid", 0) + 1
        g = self.nc.psum_tensor(f"{name}_u{self._uid}", shape, dtype)
        t = g.__enter__()
        self._scopes[-1].append(g)
        return t

    def push_scope(self):
        self._scopes.append([])

    def pop_scope(self):
        self.barrier()
        for g in reversed(self._scopes.pop()):
            g.__exit__(None, None, None)

    def close(self):
        while self._scopes:
            for g in reversed(self._scopes.pop()):
                g.__exit__(None, None, None)
        for g in reversed(self._semctx):
            g.__exit__(None, None, None)

    def _wait(self, engine, token):
        if token is None:
            return
        s, v, src_eng = token
        if src_eng == engine and engine == "tensor":
            return
        if self.waited[engine].get(s, 0) >= v:
            return
        self.waited[engine][s] = v
        h = self.sems[s][0]
        self.events[engine].append(lambda eng, h=h, v=v: eng.wait_ge(h, v))

    def _dep(self, engine, reads, writes):
        for b in reads:
            self._wait(engine, self.wr.get(id(b)))
        for b in writes:
            self._wait(engine, self.wr.get(id(b)))
            for t in self.rd.get(id(b), ()):
                self._wait(engine, t)

    def _record(self, token, reads, writes):
        for b in writes:
            self.wr[id(b)] = token
            self.rd[id(b)] = []
        for b in reads:
            self.rd.setdefault(id(b), []).append(token)

    def dma(self, queue, out, in_, rbuf, wbuf, transpose=False, slow=False):
        self._dep(queue, rbuf, wbuf)
        s = self.QSEM[queue]
        h = self.sems[s][0]
        self.sems[s][1] += 16
        tok = (s, self.sems[s][1], queue)

        def f(eng, o=out, i=in_, t=transpose, sl=slow, h=h):
            kw = {}
            if t:
                kw["transpose"] = True
            if sl:
                kw["allow_slow_non_contiguous"] = True
            return eng.dma_start(out=o, in_=i, **kw).then_inc(h, 16)
        self.events[queue].append(f)
        self._record(tok, rbuf, wbuf)
        return tok

    def op(self, engine, fn, rbuf=(), wbuf=()):
        self._dep(engine, rbuf, wbuf)
        s = self.ESEM[engine]
        h = self.sems[s][0]
        self.sems[s][1] += 1
        tok = (s, self.sems[s][1], engine)

        def f(eng, fn=fn, h=h):
            return fn(eng).then_inc(h, 1)
        self.events[engine].append(f)
        self._record(tok, rbuf, wbuf)
        return tok

    def now(self, sem):
        return (sem, self.sems[sem][1], None)

    def raw_wait(self, engine, token):
        self._wait(engine, token)

    def barrier(self):
        toks = {}
        for e in ("vector", "gpsimd"):
            toks[e] = self.op(e, lambda eng: eng.engine_nop())
        toks["scalar"] = self.op(
            "scalar", lambda eng: eng.copy(self.bscr[0:1, 0:1], self.bscr[0:1, 1:2]))
        toks["tensor"] = self.op("tensor", lambda eng: eng.nop())
        dma_toks = [self.now(s) for s in ("dS", "dG", "dA")]
        for e in self.ENGINES:
            for e2, t in toks.items():
                if e2 != e:
                    self._wait(e, t)
            for t in dma_toks:
                self._wait(e, t)

    def build(self):
        nc = self.nc
        with nc.Block() as block:
            @block.sync
            def _(eng):
                for fn in self.events["sync"]:
                    fn(eng)

            @block.vector
            def _(eng):
                for fn in self.events["vector"]:
                    fn(eng)

            @block.scalar
            def _(eng):
                for fn in self.events["scalar"]:
                    fn(eng)

            @block.tensor
            def _(eng):
                for fn in self.events["tensor"]:
                    fn(eng)

            @block.gpsimd
            def _(eng):
                for fn in self.events["gpsimd"]:
                    fn(eng)


def ap_(t, off, dims):
    return bass.AP(t, off, dims)


def emit_weight_prep(k, E, C):
    cb = C["cb"]
    k.push_scope()
    wld = [k.sb(f"wp_ld{i}", [128, INNER], F32) for i in range(2)]
    tsum = k.sb("wp_ts", [128, 32], F32)
    csum = k.sb("wp_cs", [1, 32], F32)
    tot = k.sb("wp_tot", [1, 1], F32)
    wide = k.sb("wp_wide", [1, 512], F32)
    scal512 = k.sb("wp_scal", [128, 512], F32)
    wqt = k.sb("wp_q", [128, INNER], BF16)

    def mk_ident(e):
        e.memset(cb["ident"][:], 0.0)
        return e.affine_select(out=cb["ident"][:], in_=cb["ident"][:],
                               compare_op=ALU.not_equal, fill=1.0, base=0,
                               pattern=[[-1, 128]], channel_multiplier=1)
    k.op("gpsimd", mk_ident, [], [cb["ident"]])
    k.op("vector", lambda e: e.memset(cb["ones1"][:], 1.0), [], [cb["ones1"]])

    mats = ((E["w_gate"], C["wq_gate"], INNER, D),
            (E["w_up"], C["wq_up"], INNER, D),
            (E["w_down"], C["wq_down"], D, INNER))
    if DBG_PREP == "m1":
        mats = mats[:1]
    for mi, (src, dst, rows, cols) in enumerate(mats):
        nt = rows // 128
        for it in range(nt):
            bb = it % 2
            k.dma("sync", wld[bb][:, 0:cols], src.ap()[it * 128:(it + 1) * 128, :],
                  [src], [wld[bb]])
            k.op("vector", lambda e, bb=bb, it=it, cols=cols: e.reduce_sum(
                tsum[:, it:it + 1], wld[bb][:, 0:cols], AX,
                apply_absolute_value=True), [wld[bb]], [tsum])
        k.op("gpsimd", lambda e, nt=nt: e.tensor_reduce(
            csum[0:1, 0:nt], tsum[:, 0:nt], AXC, ALU.add), [tsum], [csum])
        k.op("vector", lambda e, nt=nt: e.reduce_sum(tot[:], csum[0:1, 0:nt], AX),
             [csum], [tot])
        k.op("vector", lambda e, mi=mi: e.tensor_scalar(
            cb["swinv"][0:1, mi:mi + 1], tot[:],
            1.0 / (INNER * D), 1e-5, ALU.mult, ALU.max), [tot], [cb["swinv"]])
        k.op("vector", lambda e, mi=mi: e.reciprocal(
            tot[:], cb["swinv"][0:1, mi:mi + 1]), [cb["swinv"]], [tot])
        k.op("vector", lambda e: e.memset(wide[:], 1.0), [], [wide])
        k.op("vector", lambda e: e.tensor_scalar(
            wide[:], wide[:], tot[0:1, 0:1], None, ALU.mult), [wide, tot], [wide])
        k.dma("sync", C["rowbc"].ap()[0:1, :], wide[:], [wide], [C["rowbc"]])
        k.dma("sync", scal512[:], ap_(C["rowbc"], 0, [[0, 128], [1, 512]]),
              [C["rowbc"]], [scal512])
        for it in range(nt):
            bb = it % 2
            k.dma("sync", wld[bb][:, 0:cols], src.ap()[it * 128:(it + 1) * 128, :],
                  [src], [wld[bb]])
            k.op("vector", lambda e, bb=bb, cols=cols: e.tensor_scalar(
                wld[bb][:, 0:cols], wld[bb][:, 0:cols], scal512[:, 0:1], None, ALU.mult),
                [wld[bb], scal512], [wld[bb]])
            k.op("vector", lambda e, bb=bb, cols=cols: e.tensor_scalar(
                wld[bb][:, 0:cols], wld[bb][:, 0:cols], MAGIC, MAGIC, ALU.add, ALU.subtract),
                [wld[bb]], [wld[bb]])
            k.op("vector", lambda e, bb=bb, cols=cols: e.tensor_scalar(
                wqt[:, 0:cols], wld[bb][:, 0:cols], 1.0, -1.0, ALU.min, ALU.max),
                [wld[bb]], [wqt])
            k.dma("sync", dst.ap()[it * 128:(it + 1) * 128, :], wqt[:, 0:cols],
                  [wqt], [dst])

    for src, dst, rows in ((E["qkv_w"], C["wb_qkv"], 3 * D),
                           (E["gate_w"], C["wb_gate"], D),
                           (E["out_w"], C["wb_out"], D)):
        nt = rows // 128
        for it in range(nt):
            bb = it % 2
            k.dma("sync", wld[bb][:, 0:D], src.ap()[it * 128:(it + 1) * 128, :],
                  [src], [wld[bb]])
            k.op("vector", lambda e, bb=bb: e.tensor_copy(wqt[:, 0:D], wld[bb][:, 0:D]),
                 [wld[bb]], [wqt])
            k.dma("sync", dst.ap()[it * 128:(it + 1) * 128, :], wqt[:, 0:D],
                  [wqt], [dst])

    k.dma("gpsimd", cb["se2"][0:64, :], E["se_extT"].ap()[:, :], [E["se_extT"]], [cb["se2"]])
    k.dma("gpsimd", cb["se2"][64:128, :], E["se_extT"].ap()[:, :], [E["se_extT"]], [cb["se2"]])

    def mk_caus(cbuf, base):
        def f(e):
            e.memset(cbuf[:], 1.0)
            e.affine_select(out=cbuf[:, 0:DNW], in_=cbuf[:, 0:DNW],
                            compare_op=ALU.is_ge, fill=0.0, base=base,
                            pattern=[[1, DNW]], channel_multiplier=1)
            return e.memset(cbuf[:, DNW:CW], 0.0)
        return f
    k.op("gpsimd", mk_caus(cb["caus0"], -256), [], [cb["caus0"]])
    k.op("gpsimd", mk_caus(cb["caus1"], -128), [], [cb["caus1"]])

    zb = k.sb("wp_zb", [128, WN], BF16)
    k.op("gpsimd", lambda e: e.memset(zb[:], 0.0), [], [zb])
    for blk in range(2 * NT):
        k.dma("sync", C["psh"].ap()[blk * 128:(blk + 1) * 128, :], zb[:], [zb], [C["psh"]])
    k.pop_scope()


def emit_ffn_pass(k, E, C, src, dst):
    cb = C["cb"]
    src_ap = src.ap()
    dst_ap = dst.ap()
    k.push_scope()
    wg8 = k.sb("f_wg8", [128, 8 * INNER], FP8)
    wu8 = k.sb("f_wu8", [128, 8 * INNER], FP8)
    wd8 = k.sb("f_wd8", [128, 32 * D], FP8)
    k.push_scope()
    tbuf = [k.sb(f"f_tb{i}", [128, INNER], BF16, is_transpose_dest=True) for i in range(2)]
    for dst8, wq, nslab, slabw in ((wg8, C["wq_gate"], 8, INNER),
                                   (wu8, C["wq_up"], 8, INNER),
                                   (wd8, C["wq_down"], 32, D)):
        for i in range(nslab):
            bb = i % 2
            k.dma("sync", tbuf[bb][:, 0:slabw], wq.ap()[:, i * 128:(i + 1) * 128],
                  [wq], [tbuf[bb]], transpose=True)
            k.op("scalar", lambda e, bb=bb, d=dst8, i=i, w=slabw:
                 e.copy(d[:, i * w:(i + 1) * w], tbuf[bb][:, 0:w]), [tbuf[bb]], [dst8])
    k.pop_scope()
    if DBG_FFN_STOP == "slabs":
        k.pop_scope()
        return

    xld = [k.sb(f"f_x{i}", [128, D], F32) for i in range(2)]
    ttile = k.sb("f_t", [128, D], F32)
    jnk = k.sb("f_jnk", [128, D], F32)
    stat = k.sb("f_stat", [128, 8], F32)
    aqarr = k.sb("f_aq", [128, 32], F32)
    xqt = k.sb("f_xqt", [128, D], BF16)
    V, S = "vector", "scalar"
    for it in range(TT):
        bb = it % 2
        x_ = xld[bb]
        k.dma("sync", x_[:], src_ap[it * 128:(it + 1) * 128, :], [src], [x_])
        k.op(S, lambda e, x_=x_: e.activation(
            jnk[:], x_[:], AF.Square, accum_out=stat[:, 1:2]), [x_], [jnk, stat])
        k.op(V, lambda e, x_=x_: e.reduce_sum(stat[:, 0:1], x_[:], AX), [x_], [stat])
        k.op(V, lambda e: e.tensor_scalar(stat[:, 2:3], stat[:, 0:1], 1.0 / D, None, ALU.mult),
             [stat], [stat])
        k.op(V, lambda e, x_=x_: e.tensor_scalar(
            ttile[:], x_[:], stat[:, 2:3], None, ALU.subtract), [x_, stat], [ttile])
        k.op(V, lambda e: e.tensor_tensor(stat[:, 3:4], stat[:, 2:3], stat[:, 2:3], ALU.mult),
             [stat], [stat])
        k.op(V, lambda e: e.tensor_scalar(stat[:, 1:2], stat[:, 1:2], 1.0 / D, None, ALU.mult),
             [stat], [stat])
        k.op(V, lambda e: e.tensor_tensor(stat[:, 3:4], stat[:, 1:2], stat[:, 3:4], ALU.subtract),
             [stat], [stat])
        k.op(V, lambda e: e.tensor_scalar(stat[:, 3:4], stat[:, 3:4], EPS, None, ALU.add),
             [stat], [stat])
        k.op(S, lambda e: e.activation(stat[:, 4:5], stat[:, 3:4], AF.Sqrt), [stat], [stat])
        k.op(V, lambda e: e.reciprocal(stat[:, 5:6], stat[:, 4:5]), [stat], [stat])
        k.op(V, lambda e: e.reduce_max(stat[:, 6:7], ttile[:], AX, apply_absolute_value=True),
             [ttile], [stat])
        k.op(V, lambda e: e.tensor_tensor(stat[:, 6:7], stat[:, 6:7], stat[:, 5:6], ALU.mult),
             [stat], [stat])
        k.op(V, lambda e: e.tensor_scalar(stat[:, 6:7], stat[:, 6:7], 1e-5, None, ALU.max),
             [stat], [stat])
        k.op(V, lambda e: e.reciprocal(stat[:, 7:8], stat[:, 6:7]), [stat], [stat])
        k.op(V, lambda e: e.tensor_tensor(stat[:, 7:8], stat[:, 7:8], stat[:, 5:6], ALU.mult),
             [stat], [stat])
        k.op(V, lambda e: e.tensor_scalar(stat[:, 7:8], stat[:, 7:8], 127.0, None, ALU.mult),
             [stat], [stat])
        k.op(V, lambda e, it=it: e.tensor_scalar(
            aqarr[:, it:it + 1], stat[:, 6:7], 1.0 / 127.0, None, ALU.mult), [stat], [aqarr])
        k.op(V, lambda e: e.tensor_scalar(ttile[:], ttile[:], stat[:, 7:8], None, ALU.mult),
             [ttile, stat], [ttile])
        k.op(V, lambda e: e.tensor_scalar(
            xqt[:], ttile[:], MAGIC, MAGIC, ALU.add, ALU.subtract), [ttile], [xqt])
        k.dma("sync", C["xq_dram"].ap()[it * 128:(it + 1) * 128, :], xqt[:],
              [xqt], [C["xq_dram"]])

    aqrow = k.sb("f_aqrow", [1, CHUNK], F32)
    gbrow = k.sb("f_gbrow", [1, CHUNK], F32)
    ubrow = k.sb("f_ubrow", [1, CHUNK], F32)
    if DBG_FFN_STOP == "xchain":
        k.pop_scope()
        return

    xqF = k.sb("f_xqF", [128, 8 * CHUNK], BF16, is_transpose_dest=True)
    xqG = k.sb("f_xqG", [128, 8 * CHUNK], BF16)
    gbt = k.sb("f_gbt", [128, CHUNK], F32)
    shbt = k.sb("f_shbt", [128, CHUNK], F32)
    idq = k.sb("f_idq", [128, 128], F32)
    mrow = k.sb("f_mrow", [1, CHUNK], F32)
    shrow = k.sb("f_shrow", [1, CHUNK], F32)
    ddrow = k.sb("f_ddrow", [1, CHUNK], F32)
    prodF = k.sb("f_prod", [128, 32 * CHUNK], F32)
    hqF = k.sb("f_hq", [128, 32 * CHUNK], BF16)
    gsb = k.sb("f_gs", [128, CHUNK], F32)
    rowm = k.sb("f_rowm", [128, CHUNK], F32)
    ddc = k.sb("f_ddc", [128, QTC], F32)
    g_ps = [k.ps(f"f_gps{i}", [128, CHUNK], F32) for i in range(2)]
    u_ps = [k.ps(f"f_ups{i}", [128, CHUNK], F32) for i in range(2)]
    d_ps = k.ps("f_dps", [128, D], F32)
    xout = xld[1]
    xres = xld[0]

    k.barrier()
    for ch in range(NCH):
        row0 = ch * CHUNK
        for cc in range(8):
            k.dma("sync", xqF[:, cc * CHUNK:(cc + 1) * CHUNK],
                  C["xq_dram"].ap()[row0:row0 + CHUNK, cc * 128:(cc + 1) * 128],
                  [C["xq_dram"]], [xqF], transpose=True)
        for qt in range(QTC):
            tile = ch * QTC + qt
            k.op(V, lambda e, tile=tile: e.tensor_scalar(
                idq[:], cb["ident"][:], aqarr[:, tile:tile + 1], None, ALU.mult),
                [cb["ident"], aqarr], [idq])
            k.op("gpsimd", lambda e, qt=qt: e.tensor_reduce(
                aqrow[0:1, qt * 128:(qt + 1) * 128], idq[:], AXC, ALU.add),
                [idq], [aqrow])
        k.op(V, lambda e: e.tensor_scalar(
            gbrow[:], aqrow[:], cb["swinv"][0:1, 0:1], None, ALU.mult),
            [aqrow, cb["swinv"]], [gbrow])
        k.op(V, lambda e: e.tensor_scalar(
            ubrow[:], aqrow[:], cb["swinv"][0:1, 1:2], None, ALU.mult),
            [aqrow, cb["swinv"]], [ubrow])
        k.dma("sync", C["rowgb"].ap()[0:1, :], gbrow[:], [gbrow], [C["rowgb"]])
        k.dma("sync", gbt[:], ap_(C["rowgb"], 0, [[0, 128], [1, CHUNK]]),
              [C["rowgb"]], [gbt])
        k.op(V, lambda e: e.tensor_tensor(
            xqG[:], ap_(xqF, 0, [[8 * CHUNK, 128], [CHUNK, 8], [1, CHUNK]]),
            ap_(gbt, 0, [[CHUNK, 128], [0, 8], [1, CHUNK]]), ALU.mult),
            [xqF, gbt], [xqG])

        for ht in range(32):
            pb = ht % 2
            gp = g_ps[pb]
            up = u_ps[pb]
            for cc in range(8):
                k.op("tensor", lambda e, gp=gp, cc=cc, ht=ht: e.matmul(
                    gp[:], wg8[:, cc * INNER + ht * 128: cc * INNER + (ht + 1) * 128],
                    xqG[:, cc * CHUNK:(cc + 1) * CHUNK], start=(cc == 0), stop=(cc == 7)),
                    [wg8, xqG] if cc in (0, 7) else [], [gp] if cc in (0, 7) else [])
            for cc in range(8):
                k.op("tensor", lambda e, up=up, cc=cc, ht=ht: e.matmul(
                    up[:], wu8[:, cc * INNER + ht * 128: cc * INNER + (ht + 1) * 128],
                    xqF[:, cc * CHUNK:(cc + 1) * CHUNK], start=(cc == 0), stop=(cc == 7)),
                    [wu8, xqF] if cc in (0, 7) else [], [up] if cc in (0, 7) else [])
            k.op(S, lambda e, gp=gp: e.activation(gsb[:], gp[:], AF.Silu), [gp], [gsb])
            k.op(V, lambda e, up=up, ht=ht: e.scalar_tensor_tensor(
                prodF[:, ht * CHUNK:(ht + 1) * CHUNK], up[:], 1.0, gsb[:],
                ALU.mult, ALU.mult), [up, gsb], [prodF])

        k.op(V, lambda e: e.tensor_reduce(
            ap_(rowm, 0, [[CHUNK, 128], [1, CHUNK], [0, 1]]),
            ap_(prodF, 0, [[32 * CHUNK, 128], [1, CHUNK], [CHUNK, 32]]),
            AX, ALU.max, apply_absolute_value=True), [prodF], [rowm])
        k.op("gpsimd", lambda e: e.tensor_reduce(mrow[:], rowm[:], AXC, ALU.max),
             [rowm], [mrow])
        k.op(V, lambda e: e.tensor_tensor(mrow[:], mrow[:], ubrow[0:1, :], ALU.mult),
             [mrow, ubrow], [mrow])
        k.op(V, lambda e: e.tensor_scalar(mrow[:], mrow[:], 1e-5, None, ALU.max),
             [mrow], [mrow])
        k.op(V, lambda e: e.reciprocal(shrow[:], mrow[:]), [mrow], [shrow])
        k.op(V, lambda e: e.tensor_tensor(shrow[:], shrow[:], ubrow[0:1, :], ALU.mult),
             [shrow, ubrow], [shrow])
        k.op(V, lambda e: e.tensor_scalar(shrow[:], shrow[:], 127.0, None, ALU.mult),
             [shrow], [shrow])
        k.op(V, lambda e: e.tensor_scalar(
            ddrow[:], mrow[:], cb["swinv"][0:1, 2:3], None, ALU.mult),
            [mrow, cb["swinv"]], [ddrow])
        k.op(V, lambda e: e.tensor_scalar(ddrow[:], ddrow[:], 0.5 / 127.0, None, ALU.mult),
             [ddrow], [ddrow])
        k.dma("sync", C["rowsh"].ap()[0:1, :], shrow[:], [shrow], [C["rowsh"]])
        k.dma("sync", shbt[:], ap_(C["rowsh"], 0, [[0, 128], [1, CHUNK]]),
              [C["rowsh"]], [shbt])
        k.dma("sync", C["rowdd"].ap()[0:1, :], ddrow[:], [ddrow], [C["rowdd"]])
        k.dma("sync", gbt[:], ap_(C["rowdd"], 0, [[0, 128], [1, CHUNK]]),
              [C["rowdd"]], [gbt])
        for qt in range(QTC):
            k.op(V, lambda e, qt=qt: e.tensor_tensor(
                idq[:], gbt[:, qt * 128:(qt + 1) * 128], cb["ident"][:], ALU.mult),
                [gbt, cb["ident"]], [idq])
            k.op(V, lambda e, qt=qt: e.reduce_sum(ddc[:, qt:qt + 1], idq[:], AX),
                 [idq], [ddc])

        k.op(V, lambda e: e.tensor_tensor(
            hqF[:], ap_(prodF, 0, [[32 * CHUNK, 128], [CHUNK, 32], [1, CHUNK]]),
            ap_(shbt, 0, [[CHUNK, 128], [0, 32], [1, CHUNK]]), ALU.mult),
            [prodF, shbt], [hqF])
        k.op(V, lambda e: e.tensor_scalar(
            hqF[:], hqF[:], MAGIC, MAGIC, ALU.add, ALU.subtract), [hqF], [hqF])

        for qt in range(QTC):
            git = ch * QTC + qt
            k.dma("sync", xres[:], src_ap[git * 128:(git + 1) * 128, :], [src], [xres])
            for half in range(2):
                for hc in range(32):
                    first = half == 0 and hc == 0
                    last = half == 1 and hc == 31
                    k.op("tensor", lambda e, qt=qt, half=half, hc=hc: e.matmul(
                        d_ps[:, half * 512:(half + 1) * 512],
                        hqF[:, hc * CHUNK + qt * 128: hc * CHUNK + qt * 128 + 128],
                        wd8[:, hc * D + half * 512: hc * D + (half + 1) * 512],
                        start=(hc == 0), stop=(hc == 31)),
                        [hqF, wd8] if (first or last) else [], [d_ps] if (first or last) else [])
            k.op(V, lambda e, qt=qt: e.tensor_scalar(
                xout[:], d_ps[:], ddc[:, qt:qt + 1], None, ALU.mult),
                [d_ps, ddc], [xout])
            k.op(V, lambda e: e.tensor_tensor(xout[:], xout[:], xres[:], ALU.add),
                 [xout, xres], [xout])
            k.dma("sync", dst_ap[git * 128:(git + 1) * 128, :], xout[:], [xout], [dst])
    k.pop_scope()


DL = [3, 4, 6, 8, 12]


def _jinv(i):
    for jf, dl in enumerate(DL):
        if dl > i:
            return jf
    return NFAR


def emit_attn(k, E, C, src, dst):
    cb = C["cb"]
    V, S, G = "vector", "scalar", "gpsimd"
    src_ap, dst_ap = src.ap(), dst.ap()
    qkvF, gfd, psh, gw = C["qkvF"], C["gateF_dram"], C["psh"], C["gw"]

    def sl(n):
        return [object() for _ in range(n)]

    s_t = sl(NT)
    s_qkv = sl(24)
    s_gf = sl(8)
    s_gw = [sl(NT), sl(NT)]
    s_psh = [sl(NT), sl(NT)]
    s_cgb = [sl(NT), sl(NT)]
    s_pn = [sl(NT), sl(NT)]
    cgb_far = [object(), object()]
    GWSZ = 128 * NT * GWP + 2 * GWP

    k.push_scope()
    ebT = k.sb("a_ebT", [128, H * CW], F32)
    bqcol = k.sb("a_bqc", [128, 24], F32)
    gbcol = k.sb("a_gbc", [128, 8], F32)
    ifgcol = k.sb("a_ifgc", [128, 8], F32)
    obT = k.sb("a_obT", [128, D], F32)
    woT = k.sb("a_woT", [128, 8 * D], BF16, is_transpose_dest=True)
    k.push_scope()
    ebsc = k.sb("a_ebs", [1, H * CW], F32)
    bc3 = k.sb("a_bc3", [128, 3 * D], F32)
    idq = k.sb("a_idq", [128, 128], F32)
    k.dma("sync", ebsc[:], ap_(E["pb_ext"], 0, [[H * CW, 1], [1, H * CW]]),
          [E["pb_ext"]], [ebsc])
    k.op(S, lambda e: e.activation(ebsc[:], ebsc[:], AF.Exp), [ebsc], [ebsc])
    k.dma("sync", C["ebrow"].ap()[0:1, :], ebsc[:], [ebsc], [C["ebrow"]])
    k.dma("sync", ebT[:], ap_(C["ebrow"], 0, [[0, 128], [1, H * CW]]),
          [C["ebrow"]], [ebT])
    for srow, col, nt in ((E["qkv_b_row"], bqcol, 24), (E["gate_b_row"], gbcol, 8),
                          (E["ifg_row"], ifgcol, 8)):
        k.dma("sync", bc3[:, 0:nt * 128], ap_(srow, 0, [[0, 128], [1, nt * 128]]),
              [srow], [bc3])
        for ft in range(nt):
            k.op(V, lambda e, ft=ft: e.tensor_tensor(
                idq[:], bc3[:, ft * 128:(ft + 1) * 128], cb["ident"][:], ALU.mult),
                [bc3, cb["ident"]], [idq])
            k.op(V, lambda e, ft=ft, col=col: e.reduce_sum(
                col[:, ft:ft + 1], idq[:], AX), [idq], [col])
    k.dma("sync", obT[:], ap_(E["out_b_row"], 0, [[0, 128], [1, D]]),
          [E["out_b_row"]], [obT])
    for cc in range(8):
        for hh in range(2):
            k.dma("sync", woT[:, cc * D + hh * 512: cc * D + (hh + 1) * 512],
                  C["wb_out"].ap()[hh * 512:(hh + 1) * 512, cc * 128:(cc + 1) * 128],
                  [C["wb_out"]], [woT], transpose=True)
    k.pop_scope()

    for b in (range(B) if DBG_B is None else DBG_B):
        r0 = b * N
        k.push_scope()
        agF = k.sb("a_agF", [128, 8 * N], BF16)

        k.push_scope()
        xld = [k.sb(f"a_x{i}", [128, D], F32) for i in range(2)]
        jnk = k.sb("a_jnk", [128, D], F32)
        ttile = k.sb("a_tt", [128, D], F32)
        stat = k.sb("a_st", [128, 8], F32)
        idq2 = k.sb("a_idq2", [128, 128], F32)
        rrow = k.sb("a_rrow", [1, N], F32)
        tb = [k.sb(f"a_tb{i}", [128, D], BF16) for i in range(2)]
        for i in range(NT):
            x_ = xld[i % 2]
            tb_ = tb[i % 2]
            k.dma("sync", x_[:], src_ap[r0 + i * 128: r0 + (i + 1) * 128, :], [src], [x_])
            k.op(S, lambda e, x_=x_: e.activation(
                jnk[:], x_[:], AF.Square, accum_out=stat[:, 1:2]), [x_], [jnk, stat])
            k.op(V, lambda e, x_=x_: e.reduce_sum(stat[:, 0:1], x_[:], AX), [x_], [stat])
            k.op(V, lambda e: e.tensor_scalar(stat[:, 2:3], stat[:, 0:1], 1.0 / D, None, ALU.mult),
                 [stat], [stat])
            k.op(V, lambda e, x_=x_: e.tensor_scalar(
                ttile[:], x_[:], stat[:, 2:3], None, ALU.subtract), [x_, stat], [ttile])
            k.op(V, lambda e: e.tensor_tensor(stat[:, 3:4], stat[:, 2:3], stat[:, 2:3], ALU.mult),
                 [stat], [stat])
            k.op(V, lambda e: e.tensor_scalar(stat[:, 1:2], stat[:, 1:2], 1.0 / D, None, ALU.mult),
                 [stat], [stat])
            k.op(V, lambda e: e.tensor_tensor(stat[:, 3:4], stat[:, 1:2], stat[:, 3:4], ALU.subtract),
                 [stat], [stat])
            k.op(V, lambda e: e.tensor_scalar(stat[:, 3:4], stat[:, 3:4], EPS, None, ALU.add),
                 [stat], [stat])
            k.op(S, lambda e: e.activation(stat[:, 4:5], stat[:, 3:4], AF.Sqrt), [stat], [stat])
            k.op(V, lambda e: e.reciprocal(stat[:, 5:6], stat[:, 4:5]), [stat], [stat])
            k.op(V, lambda e, tb_=tb_: e.tensor_copy(tb_[:], ttile[:]), [ttile], [tb_])
            k.dma("sync", C["t_dram"].ap()[i * 128:(i + 1) * 128, :], tb_[:], [tb_], [s_t[i]])
            k.op(V, lambda e: e.tensor_scalar(
                idq2[:], cb["ident"][:], stat[:, 5:6], None, ALU.mult),
                [cb["ident"], stat], [idq2])
            k.op(G, lambda e, i=i: e.tensor_reduce(
                rrow[0:1, i * 128:(i + 1) * 128], idq2[:], AXC, ALU.add), [idq2], [rrow])
        k.dma("sync", C["rowr"].ap()[0:1, :], rrow[:], [rrow], [C["rowr"]])
        k.pop_scope()

        k.push_scope()
        tF = k.sb("a_tF", [128, 8 * N], BF16, is_transpose_dest=True)
        wqT = k.sb("a_wqT", [128, 8 * 3 * D], BF16, is_transpose_dest=True)
        wgT = k.sb("a_wgT", [128, 8 * D], BF16, is_transpose_dest=True)
        rbcT = k.sb("a_rbc", [128, N], F32)
        ev1 = [k.sb(f"a_ev1{i}", [128, 512], F32) for i in range(2)]
        evb = [k.sb(f"a_evb{i}", [128, 512], BF16) for i in range(2)]
        pr_ps = [k.ps(f"a_pps{i}", [128, 512], F32) for i in range(2)]
        k.dma("sync", rbcT[:], ap_(C["rowr"], 0, [[0, 128], [1, N]]), [C["rowr"]], [rbcT])
        for cc in range(8):
            k.dma("sync", tF[:, cc * N:(cc + 1) * N],
                  C["t_dram"].ap()[:, cc * 128:(cc + 1) * 128],
                  list(s_t), [tF], transpose=True)
            for hh in range(2):
                k.dma("sync", wqT[:, cc * 3 * D + hh * 1536: cc * 3 * D + (hh + 1) * 1536],
                      C["wb_qkv"].ap()[hh * 1536:(hh + 1) * 1536, cc * 128:(cc + 1) * 128],
                      [C["wb_qkv"]], [wqT], transpose=True)
            k.dma("sync", wgT[:, cc * D:(cc + 1) * D],
                  C["wb_gate"].ap()[:, cc * 128:(cc + 1) * 128],
                  [C["wb_gate"]], [wgT], transpose=True)
        for ft in range(24):
            for g in range(4):
                pp = pr_ps[g % 2]
                pe = ev1[g % 2]
                pq = evb[g % 2]
                for cc in range(8):
                    k.op("tensor", lambda e, pp=pp, cc=cc, ft=ft, g=g: e.matmul(
                        pp[:], wqT[:, cc * 3 * D + ft * 128: cc * 3 * D + (ft + 1) * 128],
                        tF[:, cc * N + g * 512: cc * N + (g + 1) * 512],
                        start=(cc == 0), stop=(cc == 7)),
                        [wqT, tF] if cc in (0, 7) else [], [pp] if cc in (0, 7) else [])
                k.op(V, lambda e, pp=pp, pe=pe, g=g: e.tensor_tensor(
                    pe[:], pp[:], rbcT[:, g * 512:(g + 1) * 512], ALU.mult),
                    [pp, rbcT], [pe])
                k.op(V, lambda e, pe=pe, pq=pq, ft=ft: e.tensor_scalar(
                    pq[:], pe[:], bqcol[:, ft:ft + 1], None, ALU.add),
                    [pe, bqcol], [pq])
                k.dma("sync", qkvF.ap()[ft * 128:(ft + 1) * 128, g * 512:(g + 1) * 512],
                      pq[:], [pq], [s_qkv[ft]])
        for ft in range(8):
            for g in range(4):
                pp = pr_ps[g % 2]
                pe = ev1[g % 2]
                pq = evb[g % 2]
                for cc in range(8):
                    k.op("tensor", lambda e, pp=pp, cc=cc, ft=ft, g=g: e.matmul(
                        pp[:], wgT[:, cc * D + ft * 128: cc * D + (ft + 1) * 128],
                        tF[:, cc * N + g * 512: cc * N + (g + 1) * 512],
                        start=(cc == 0), stop=(cc == 7)),
                        [wgT, tF] if cc in (0, 7) else [], [pp] if cc in (0, 7) else [])
                k.op(V, lambda e, pp=pp, pe=pe, g=g: e.tensor_tensor(
                    pe[:], pp[:], rbcT[:, g * 512:(g + 1) * 512], ALU.mult),
                    [pp, rbcT], [pe])
                k.op(S, lambda e, pe=pe, ft=ft: e.activation(
                    pe[:], pe[:], AF.Sigmoid, bias=gbcol[:, ft:ft + 1], scale=1.0),
                    [pe, gbcol], [pe])
                k.op(V, lambda e, pe=pe, pq=pq, ft=ft: e.tensor_scalar(
                    pq[:], pe[:], ifgcol[:, ft:ft + 1], None, ALU.mult),
                    [pe, ifgcol], [pq])
                k.dma("sync", gfd.ap()[ft * 128:(ft + 1) * 128, g * 512:(g + 1) * 512],
                      pq[:], [pq], [s_gf[ft]])
        k.pop_scope()

        k.push_scope()
        qtm = [k.sb(f"a_qtm{j}", [128, NT * 64], BF16, is_transpose_dest=True) for j in range(2)]
        ktm = [k.sb(f"a_ktm{j}", [128, NT * 64], BF16, is_transpose_dest=True) for j in range(2)]
        vtm = [k.sb(f"a_vtm{j}", [128, NT * 64], BF16, is_transpose_dest=True) for j in range(2)]
        QF = [k.sb(f"a_QF{j}", [64, N], BF16) for j in range(2)]
        KFp = [k.sb(f"a_KFp{j}", [64, N + 256], BF16) for j in range(2)]
        gf = [k.sb(f"a_gf{j}", [64, N], BF16) for j in range(2)]
        cgb = [k.sb(f"a_cgb{j}", [128, NT * CW], BF16) for j in range(2)]
        pnall = [k.sb(f"a_pn{j}", [128, NT * CW], BF16) for j in range(2)]
        fqt = k.sb("a_fqt", [128, 13 * 64], F32)
        fqr = k.sb("a_fqr", [128, NT], F32)
        ptb = [k.sb(f"a_ptb{i}", [128, CW], F32) for i in range(2)]
        wsb = [k.sb(f"a_wsb{i}", [128, WN], BF16) for i in range(2)]
        zcol = k.sb("a_z", [128, 4], F32)
        pf = [k.sb(f"a_pf{j}", [128, NT * 8], F32) for j in range(2)]
        diagb = [k.sb(f"a_dg{i}", [128, 128], BF16) for i in range(4)]
        agh = [k.sb(f"a_agh{j}", [64, N], BF16) for j in range(2)]
        PT = [k.sb(f"a_PT{x}", [128, N], BF16, is_transpose_dest=True) for x in range(3)]
        win_ps = [k.ps(f"a_wps{i}", [128, WN], F32) for i in range(2)]
        qs_ps = [k.ps(f"a_qps{i}", [128, CW], F32) for i in range(2)]
        pvp = [k.ps(f"a_pvp{i}", [64, 128], F32) for i in range(2)]
        for j in range(2):
            k.op(V, lambda e, j=j: e.memset(KFp[j][:, 0:256], 0.0), [], [KFp[j]])
            k.op(V, lambda e, j=j: e.memset(cgb[j][:], 0.0), [],
                 [cgb[j]] + list(s_cgb[j]) + [cgb_far[j]])

        for h in range(H):
            hp = h % 2
            hq, hk, hv = h * 64, D + h * 64, 2 * D + h * 64
            sq, sk, sv = s_qkv[h // 2], s_qkv[8 + h // 2], s_qkv[16 + h // 2]
            qt_, kt_, vt_ = qtm[hp], ktm[hp], vtm[hp]
            QF_, KF_, gf_ = QF[hp], KFp[hp], gf[hp]
            cgb_, pn_, pf_, agh_ = cgb[hp], pnall[hp], pf[hp], agh[hp]
            gwo = hp * GWSZ
            psr = hp * N
            for i in range(NT):
                k.dma("sync", qt_[:, i * 64:(i + 1) * 64],
                      qkvF.ap()[hq:hq + 64, i * 128:(i + 1) * 128], [sq], [qt_], transpose=True)
                k.dma("sync", kt_[:, i * 64:(i + 1) * 64],
                      qkvF.ap()[hk:hk + 64, i * 128:(i + 1) * 128], [sk], [kt_], transpose=True)
                k.dma("sync", vt_[:, i * 64:(i + 1) * 64],
                      qkvF.ap()[hv:hv + 64, i * 128:(i + 1) * 128], [sv], [vt_], transpose=True)
            k.dma("gpsimd", QF_[0:64, :], qkvF.ap()[hq:hq + 64, :], [sq], [QF_])
            k.dma("gpsimd", KF_[0:64, 256:256 + N], qkvF.ap()[hk:hk + 64, :], [sk], [KF_])
            k.dma("gpsimd", gf_[0:64, :], gfd.ap()[hq:hq + 64, :], [s_gf[h // 2]], [gf_])

            for jf, dl in enumerate(DL):
                w = (NT - dl) * 64
                k.op(V, lambda e, dl=dl, w=w, qt_=qt_, kt_=kt_: e.tensor_tensor(
                    fqt[:, 0:w], qt_[:, dl * 64:NT * 64], kt_[:, 0:w], ALU.mult),
                    [qt_, kt_], [fqt])
                k.op(V, lambda e, dl=dl: e.reduce_sum(
                    fqr[:, 0:NT - dl],
                    ap_(fqt, 0, [[13 * 64, 128], [64, NT - dl], [1, 64]]), AX),
                    [fqt], [fqr])
                k.op(V, lambda e, jf=jf, dl=dl, cgb_=cgb_: e.tensor_copy(
                    ap_(cgb_, dl * CW + DNW + jf, [[NT * CW, 128], [CW, NT - dl]]),
                    fqr[:, 0:NT - dl]), [fqr], [cgb_far[hp]])

            for i in range(NT):
                pp = i % 2
                k.op("tensor", lambda e, i=i, pp=pp, QF_=QF_, KF_=KF_: e.matmul(
                    win_ps[pp][:], QF_[0:64, i * 128:(i + 1) * 128],
                    KF_[0:64, i * 128: i * 128 + WN], start=True, stop=True),
                    [QF_, KF_], [win_ps[pp]])
                k.op("tensor", lambda e, i=i, pp=pp, QF_=QF_: e.matmul(
                    qs_ps[pp][:], QF_[0:64, i * 128:(i + 1) * 128],
                    cb["se2"][0:64, :], start=True, stop=True),
                    [QF_, cb["se2"]], [qs_ps[pp]])
                k.op(V, lambda e, pp=pp: e.tensor_copy(wsb[pp][:], win_ps[pp][:]),
                     [win_ps[pp]], [wsb[pp]])
                k.dma("scalar", ap_(gw, gwo + i * GWP, [[NT * GWP - 1, 128], [1, WN]]),
                      wsb[pp][:], [wsb[pp]], [s_gw[hp][i]])
                k.dma("scalar", cgb_[:, i * CW: i * CW + DNW],
                      ap_(gw, gwo + i * GWP, [[NT * GWP, 128], [1, DNW]]),
                      [s_gw[hp][i]], [s_cgb[hp][i]])
                k.op(V, lambda e, i=i, pp=pp, cgb_=cgb_: e.scalar_tensor_tensor(
                    ptb[pp][:], qs_ps[pp][:], 1.0, cgb_[:, i * CW:(i + 1) * CW],
                    ALU.add, ALU.mult), [qs_ps[pp], s_cgb[hp][i], cgb_far[hp]], [ptb[pp]])
                k.op(S, lambda e, pp=pp: e.activation(
                    ptb[pp][:], ptb[pp][:], AF.Exp, scale=INVS), [ptb[pp]], [ptb[pp]])
                k.op(V, lambda e, pp=pp, h=h: e.tensor_tensor(
                    ptb[pp][:], ptb[pp][:], ebT[:, h * CW:(h + 1) * CW], ALU.mult),
                    [ptb[pp], ebT], [ptb[pp]])
                if i <= 1:
                    cm = cb["caus0"] if i == 0 else cb["caus1"]
                    k.op(V, lambda e, pp=pp, cm=cm: e.tensor_tensor(
                        ptb[pp][:], ptb[pp][:], cm[:], ALU.mult), [ptb[pp], cm], [ptb[pp]])
                elif _jinv(i) < NFAR:
                    jv = _jinv(i)
                    k.op(V, lambda e, pp=pp, jv=jv: e.memset(
                        ptb[pp][:, DNW + jv:DNW + NFAR], 0.0), [ptb[pp]], [ptb[pp]])
                k.op(V, lambda e, pp=pp: e.reduce_sum(
                    zcol[:, pp:pp + 1], ptb[pp][:], AX), [ptb[pp]], [zcol])
                k.op(V, lambda e, pp=pp: e.reciprocal(
                    zcol[:, 2 + pp:3 + pp], zcol[:, pp:pp + 1]), [zcol], [zcol])
                k.op(V, lambda e, i=i, pp=pp, pn_=pn_: e.tensor_scalar(
                    pn_[:, i * CW:(i + 1) * CW], ptb[pp][:],
                    zcol[:, 2 + pp:3 + pp], None, ALU.mult),
                    [ptb[pp], zcol], [s_pn[hp][i]])
                k.op(V, lambda e, i=i, pp=pp, pf_=pf_: e.tensor_scalar(
                    pf_[:, i * 8:i * 8 + NFAR], ptb[pp][:, DNW:DNW + NFAR],
                    zcol[:, 2 + pp:3 + pp], None, ALU.mult),
                    [ptb[pp], zcol], [pf_])
                k.dma("gpsimd", ap_(psh, (psr + i * 128) * WN, [[WN + 1, 128], [1, DNW]]),
                      pn_[:, i * CW: i * CW + DNW], [s_pn[hp][i]], [s_psh[hp][i]])

            for x in range(3):
                k.dma("sync", PT[x][:], psh.ap()[psr:psr + N, x * 128:(x + 1) * 128],
                      list(s_psh[hp]), [PT[x]], transpose=True)

            nd = 0
            for i in range(NT):
                pq = pvp[i % 2]
                chain = [("far", i - dl, jf) for jf, dl in enumerate(DL) if i - dl >= 0]
                chain += [("win", t, t + 2 - i) for t in (i - 2, i - 1, i) if t >= 0]
                for ci, (kind, t, aux) in enumerate(chain):
                    st_ = ci == 0
                    sp_ = ci == len(chain) - 1
                    if kind == "far":
                        db = diagb[nd % 4]
                        nd += 1
                        k.op(V, lambda e, db=db, i=i, aux=aux, pf_=pf_: e.tensor_scalar(
                            db[:], cb["ident"][:], pf_[:, i * 8 + aux:i * 8 + aux + 1],
                            None, ALU.mult), [cb["ident"], pf_], [db])
                        k.op("tensor", lambda e, pq=pq, t=t, db=db, st_=st_, sp_=sp_, vt_=vt_: e.matmul(
                            pq[:], vt_[:, t * 64:(t + 1) * 64], db[:],
                            start=st_, stop=sp_), [vt_, db], [pq])
                    else:
                        k.op("tensor", lambda e, pq=pq, t=t, i=i, aux=aux, st_=st_, sp_=sp_, vt_=vt_: e.matmul(
                            pq[:], vt_[:, t * 64:(t + 1) * 64],
                            PT[aux][:, i * 128:(i + 1) * 128],
                            start=st_, stop=sp_), [vt_, PT[aux]], [pq])
                k.op(V, lambda e, pq=pq, i=i, gf_=gf_, agh_=agh_: e.tensor_tensor(
                    agh_[:, i * 128:(i + 1) * 128], pq[:],
                    gf_[0:64, i * 128:(i + 1) * 128], ALU.mult), [pq, gf_], [agh_])
            k.dma("scalar",
                  agF[(h % 2) * 64:(h % 2) * 64 + 64, (h // 2) * N:(h // 2 + 1) * N],
                  agh_[0:64, :], [agh_], [agF])
        k.pop_scope()
        if DBG_AGF is not None:
            for j in range(16):
                k.dma("gpsimd", DBG_AGF.ap()[j * 128:(j + 1) * 128, :],
                      agF[:, j * 1024:(j + 1) * 1024], [agF], [DBG_AGF])

        k.push_scope()
        op_ps = [k.ps(f"a_ops{i}", [128, 512], F32) for i in range(2)]
        oe = [k.sb(f"a_oe{i}", [128, 512], F32) for i in range(2)]
        x1t = [k.sb(f"a_x1t{i}", [128, D], F32) for i in range(2)]
        xo = [k.sb(f"a_xo{i}", [128, D], F32) for i in range(2)]
        for i in range(NT):
            xr = x1t[i % 2]
            xO = xo[i % 2]
            k.dma("sync", xr[:], src_ap[r0 + i * 128: r0 + (i + 1) * 128, :], [src], [xr])
            for g in range(2):
                pp = op_ps[g]
                for cc in range(8):
                    k.op("tensor", lambda e, pp=pp, cc=cc, i=i, g=g: e.matmul(
                        pp[:], agF[:, cc * N + i * 128: cc * N + (i + 1) * 128],
                        woT[:, cc * D + g * 512: cc * D + (g + 1) * 512],
                        start=(cc == 0), stop=(cc == 7)),
                        [agF, woT] if cc in (0, 7) else [], [pp] if cc in (0, 7) else [])
                k.op(V, lambda e, pp=pp, g=g: e.tensor_tensor(
                    oe[g][:], pp[:], obT[:, g * 512:(g + 1) * 512], ALU.add),
                    [pp, obT], [oe[g]])
                k.op(V, lambda e, g=g, xO=xO, xr=xr: e.tensor_tensor(
                    xO[:, g * 512:(g + 1) * 512], oe[g][:],
                    xr[:, g * 512:(g + 1) * 512], ALU.add), [oe[g], xr], [xO])
            k.dma("sync", dst_ap[r0 + i * 128: r0 + (i + 1) * 128, :], xO[:], [xO], [dst])
        k.pop_scope()
        k.pop_scope()
    k.pop_scope()


def build_core_program(stop_after=None):
    nc = bass.Bass("TRN2", target_bir_lowering=False)
    k = K(nc)

    E = {}
    E["x0"] = nc.dram_tensor("x0", [TOK, D], F32, kind="ExternalInput")
    out = nc.dram_tensor("outp", [TOK, D], F32, kind="ExternalOutput")
    for nm, shp in (("w_gate", [INNER, D]), ("w_up", [INNER, D]), ("w_down", [D, INNER]),
                    ("qkv_w", [3 * D, D]), ("out_w", [D, D]), ("gate_w", [D, D]),
                    ("se_extT", [HD, CW]), ("pb_ext", [H, CW]), ("ifg_row", [1, D]),
                    ("qkv_b_row", [1, 3 * D]), ("gate_b_row", [1, D]), ("out_b_row", [1, D])):
        E[nm] = nc.dram_tensor(nm, shp, F32, kind="ExternalInput")

    C = {}
    C["x1"] = nc.dram_tensor("x1", [TOK, D], F32)
    C["x2"] = nc.dram_tensor("x2", [TOK, D], F32)
    C["xq_dram"] = nc.dram_tensor("xq_dram", [TOK, D], BF16)
    C["wq_gate"] = nc.dram_tensor("wq_gate", [INNER, D], BF16)
    C["wq_up"] = nc.dram_tensor("wq_up", [INNER, D], BF16)
    C["wq_down"] = nc.dram_tensor("wq_down", [D, INNER], BF16)
    C["wb_qkv"] = nc.dram_tensor("wb_qkv", [3 * D, D], BF16)
    C["wb_out"] = nc.dram_tensor("wb_out", [D, D], BF16)
    C["wb_gate"] = nc.dram_tensor("wb_gate", [D, D], BF16)
    C["t_dram"] = nc.dram_tensor("t_dram", [N, D], BF16)
    C["qkvF"] = nc.dram_tensor("qkvF", [3 * D, N], BF16)
    C["gateF_dram"] = nc.dram_tensor("gateF_dram", [D, N], BF16)
    C["gw"] = nc.dram_tensor("gw", [1, 2 * (128 * NT * GWP + 2 * GWP)], BF16)
    C["psh"] = nc.dram_tensor("psh", [2 * N, WN], BF16)
    C["ebrow"] = nc.dram_tensor("ebrow", [1, H * CW], F32)
    C["rowr"] = nc.dram_tensor("rowr", [1, N], F32)
    C["rowbc"] = nc.dram_tensor("rowbc", [1, 512], F32)
    C["rowgb"] = nc.dram_tensor("rowgb", [1, CHUNK], F32)
    C["rowsh"] = nc.dram_tensor("rowsh", [1, CHUNK], F32)
    C["rowdd"] = nc.dram_tensor("rowdd", [1, CHUNK], F32)





    cb = {}
    k.bscr = k.sb("cb_bscr", [1, 2], F32)
    cb["swinv"] = k.sb("cb_swinv", [1, 4], F32)
    cb["se2"] = k.sb("cb_se2", [128, CW], BF16)
    cb["ident"] = k.sb("cb_ident", [128, 128], F32)
    cb["ones1"] = k.sb("cb_ones1", [1, 128], F32)
    cb["caus0"] = k.sb("cb_caus0", [128, CW], F32)
    cb["caus1"] = k.sb("cb_caus1", [128, CW], F32)
    C["cb"] = cb

    emit_weight_prep(k, E, C)
    k.barrier()
    if stop_after and stop_after.startswith("dump:"):
        name = stop_after.split(":")[1]
        if name != "wq_gate":
            emit_ffn_pass(k, E, C, E["x0"], C["x1"])
        k.barrier()
        k.push_scope()
        dbuf = k.sb("dbg", [128, D], F32)
        srcb = C[name]
        nrows = {"wq_gate": INNER, "xq_dram": TOK, "x1": TOK}[name]
        for it in range(min(nrows, TOK) // 128):
            k.dma("gpsimd", dbuf[:], srcb.ap()[it * 128:(it + 1) * 128, 0:D],
                  [srcb], [dbuf])
            k.dma("gpsimd", out.ap()[it * 128:(it + 1) * 128, :], dbuf[:], [dbuf], [out])
        k.pop_scope()
    elif stop_after == "ffn1":
        emit_ffn_pass(k, E, C, E["x0"], out)
    elif stop_after == "attnx0":
        emit_attn(k, E, C, E["x0"], out)
    elif stop_after and stop_after.startswith("adump:"):
        name = stop_after.split(":")[1]
        global DBG_AGF
        if name == "agF":
            C["dbgag"] = nc.dram_tensor("dbgag", [TOK, D], BF16)
            DBG_AGF = C["dbgag"]
        emit_attn(k, E, C, E["x0"], C["x2"])
        DBG_AGF = None
        if name == "agF":
            name = "dbgag"
        k.barrier()
        k.push_scope()
        dbuf = k.sb("adg", [128, D], F32)
        srcb = C[name]
        shps = {"qkvF": (3 * D, N), "gateF_dram": (D, N), "psh": (N, WN),
                "t_dram": (N, D), "x1": (TOK, D), "x2": (TOK, D), "dbgag": (TOK, D)}
        rr, cc_ = shps[name]
        ccw = min(cc_, D)
        for it in range(min(rr, TOK) // 128):
            k.dma("gpsimd", dbuf[:, 0:ccw], srcb.ap()[it * 128:(it + 1) * 128, 0:ccw],
                  [srcb], [dbuf])
            k.dma("gpsimd", out.ap()[it * 128:(it + 1) * 128, :], dbuf[:], [dbuf], [out])
        k.pop_scope()
    else:
        emit_ffn_pass(k, E, C, E["x0"], C["x1"])
        k.barrier()
        emit_attn(k, E, C, C["x1"], out if stop_after == "attn" else C["x2"])
        if stop_after != "attn":
            k.barrier()
            emit_ffn_pass(k, E, C, C["x2"], out)

    k.raw_wait("sync", k.now("dS"))
    k.raw_wait("gpsimd", k.now("dG"))
    k.raw_wait("scalar", k.now("dA"))

    k.build()
    k.close()
    return nc


def host_prep(inputs):
    L = inputs
    se = np.asarray(L["L0_attn_scale_embed"], np.float32)
    pb = np.asarray(L["L0_attn_pos_bias"], np.float32)
    se_extT = np.zeros((HD, CW), np.float32)
    pb_ext = np.full((H, CW), -1e4, np.float32)
    omap = {o: i for i, o in enumerate(OFFSETS)}
    for j in range(DNW):
        o = 256 - j
        if o in omap:
            se_extT[:, j] = se[omap[o]]
            pb_ext[:, j] = pb[omap[o]]
    for jf, o in enumerate(FAR_O):
        se_extT[:, DNW + jf] = se[omap[o]]
        pb_ext[:, DNW + jf] = pb[omap[o]]
    ifg = np.asarray(L["L0_attn_if_gain"], np.float32)
    ifg_row = np.ascontiguousarray(np.repeat(ifg, HD)[None, :])
    return dict(
        w_gate=np.ascontiguousarray(np.asarray(L["L0_ffn_w_gate"], np.float32)),
        w_up=np.ascontiguousarray(np.asarray(L["L0_ffn_w_up"], np.float32)),
        w_down=np.ascontiguousarray(np.asarray(L["L0_ffn_w_down"], np.float32)),
        qkv_w=np.ascontiguousarray(np.asarray(L["L0_attn_qkv_w"], np.float32)),
        out_w=np.ascontiguousarray(np.asarray(L["L0_attn_out_w"], np.float32)),
        gate_w=np.ascontiguousarray(np.asarray(L["L0_attn_gate_w"], np.float32)),
        se_extT=se_extT, pb_ext=pb_ext, ifg_row=ifg_row,
        qkv_b_row=np.ascontiguousarray(np.asarray(L["L0_attn_qkv_b"], np.float32)[None, :]),
        gate_b_row=np.ascontiguousarray(np.asarray(L["L0_attn_gate_b"], np.float32)[None, :]),
        out_b_row=np.ascontiguousarray(np.asarray(L["L0_attn_out_b"], np.float32)[None, :]),
    )


def kernel(**inputs):
    x = np.ascontiguousarray(np.asarray(inputs["L0_attn_x"], np.float32))
    shared = host_prep(inputs)
    nc = build_core_program()
    in_maps = []
    for c in range(8):
        m = dict(shared)
        m["x0"] = np.ascontiguousarray(x[c * B:(c + 1) * B].reshape(TOK, D))
        in_maps.append(m)
    res = run_bass_kernel_spmd(nc, in_maps, core_ids=list(range(8)))
    outs = np.stack([np.asarray(res.results[c]["outp"]).reshape(B, N, D) for c in range(8)])
    return outs.reshape(16, N, D).astype(np.float32)
